# version 09 (final)
# baseline (speedup 1.0000x reference)
import contextlib
import numpy as np
import ml_dtypes

import concourse.bass as bass
import concourse.mybir as mybir
from concourse.bass_utils import run_bass_kernel_spmd

F32 = mybir.dt.float32
BF16 = mybir.dt.bfloat16
I32 = mybir.dt.int32
I8 = mybir.dt.int8
AF = mybir.ActivationFunctionType
ALU = mybir.AluOpType

NCORE = 8
D = 4096
NGD = 32
TLOC = 1024
NPIECE = 8
BLK = 128
DIN = 8192
DLOC = 1024
NGM = 8
NST = 16
RNK = 256
EPS_LN = 1e-5
EPS_RMS = 1e-6
G = 512

STAGE = "full"

SEMS = ("ld", "gld", "pe", "vx", "sx", "gx", "cc", "st", "gst",
        "lda", "ldb", "ldr", "fl", "rst", "wmd", "w1d", "wqd")
ENGS = ("sync", "tensor", "vector", "scalar", "gpsimd")


class P:
    def __init__(self, cur, eng, sem):
        self.cur = cur
        self.e = eng
        self.s = sem
        self.c = {k: 0 for k in SEMS}

    def dma(self, eng, out, in_, inc, slow=False):
        if eng == self.cur:
            self.e[eng].dma_start(
                out=out, in_=in_,
                allow_slow_non_contiguous=slow).then_inc(self.s[inc], 16)
        self.c[inc] += 16

    def op(self, eng, meth, *a, inc=None, **k):
        if eng == self.cur:
            ins = getattr(self.e[eng], meth)(*a, **k)
            if inc is not None:
                ins.then_inc(self.s[inc], 1)
        if inc is not None:
            self.c[inc] += 1

    def mm(self, *a, inc=None, **k):
        self.op("tensor", "matmul", *a, inc=inc, **k)

    def coll(self, kind, alu, in_ap, out_ap):
        if self.cur == "gpsimd":
            self.e["gpsimd"].collective_compute(
                kind, alu, replica_groups=[list(range(NCORE))],
                ins=[in_ap], outs=[out_ap]).then_inc(self.s["cc"], 1)
        self.c["cc"] += 1

    def wait(self, eng, sem, val=None):
        if val is None:
            val = self.c[sem]
        if eng == self.cur:
            self.e[eng].wait_ge(self.s[sem], val)

    def mark(self, sem):
        return self.c[sem]

    def drain_flag(self):
        if self.cur == "sync":
            self.e["sync"].drain()
            self.e["sync"].sem_inc(self.s["fl"], 1)
        self.c["fl"] += 1

    def gdrain(self):
        if self.cur == "gpsimd":
            self.e["gpsimd"].drain()

    def barrier(self, engs=ENGS):
        for e in engs:
            for sem in ("pe", "vx", "sx", "gx", "ld", "gld", "st", "gst",
                        "lda", "ldb", "ldr", "rst", "wmd", "w1d", "wqd"):
                self.wait(e, sem)



def _token_idx(core):
    idx = np.arange(TLOC)
    return 1024 * (idx // BLK) + BLK * core + (idx % BLK)


def _shard_inputs(inp):
    xT = np.ascontiguousarray(np.asarray(inp["L0_attn_data"])[0].T)
    wT = np.ascontiguousarray(np.asarray(inp["L0_attn_weight"]).T)
    mT = np.ascontiguousarray(np.asarray(inp["L0_attn_mask"]).T)
    w1T = np.ascontiguousarray(np.asarray(inp["L0_ffn_W_base"]).T)
    rT = np.ascontiguousarray(np.asarray(inp["L0_ffn_W_router"]).T)
    AT = np.ascontiguousarray(
        np.asarray(inp["L0_ffn_A"]).transpose(2, 0, 1).reshape(D, 128))
    BT = np.ascontiguousarray(
        np.asarray(inp["L0_ffn_B"]).transpose(0, 2, 1).reshape(128, D))
    b2 = np.ascontiguousarray(
        np.asarray(inp["L0_ffn_b_base"]).reshape(NGD, 128).T)
    ip = np.asarray(inp["L1_attn_in_proj_w"])
    cw = np.asarray(inp["L1_attn_conv_w"])
    cb = np.asarray(inp["L1_attn_conv_b"])
    xpT = np.ascontiguousarray(np.asarray(inp["L1_attn_x_proj_w"]).T)
    dtpT = np.ascontiguousarray(np.asarray(inp["L1_attn_dt_proj_w"]).T)
    dtb = np.asarray(inp["L1_attn_dt_bias"])
    alog = np.asarray(inp["L1_attn_A_log"])
    dpar = np.asarray(inp["L1_attn_D_param"])
    opT = np.ascontiguousarray(np.asarray(inp["L1_attn_out_proj_w"]).T)
    tT = np.ascontiguousarray(np.asarray(inp["L1_ffn_ternary"]).T)
    s2 = np.ascontiguousarray(np.asarray(inp["L1_ffn_scales"]).reshape(D, 32).T)

    def gmaj(v, ng):
        v = v.reshape(ng, 128, -1).transpose(1, 0, 2)
        return np.ascontiguousarray(v.reshape(128, -1))

    sel16 = np.zeros((16, 16 * 128), ml_dtypes.bfloat16)
    for n in range(16):
        sel16[n, n * 128:(n + 1) * 128] = 1.0
    pat = np.zeros((8, 128), np.float32)
    for e in range(8):
        pat[e, e * 16:(e + 1) * 16] = 2.0
    cv = np.zeros((128, 6), np.float32)
    cv[:, 0] = EPS_LN
    cv[:, 1] = EPS_RMS
    cv[:, 2] = -0.5
    cv[:, 3] = 1.0 / 4096.0
    cv[:, 4] = 1.0 / 256.0
    cv[:, 5] = 1.0 / 16.0
    consts = {
        "cvals": cv,
        "ones_col_f": np.ones((128, 1), np.float32),
        "ones_row_f": np.ones((1, 128), np.float32),
        "ones16_f": np.ones((16, 1), np.float32),
        "sel16b": sel16,
        "pat_er": pat,
    }

    s2f = np.stack([np.ascontiguousarray(
        s2[:, oc * 128:(oc + 1) * 128]).reshape(-1)
        for oc in range(32)])

    in_maps = []
    for c in range(NCORE):
        ti = _token_idx(c)
        mc0 = DLOC * c
        cols_ip = np.concatenate([np.arange(mc0, mc0 + DLOC),
                                  DIN + np.arange(mc0, mc0 + DLOC)])
        m = {
            "x0c": xT[:, ti],
            "wTf": wT,
            "mTf": mT,
            "w1Tf": w1T,
            "tTf": tT,
            "s2f": s2f,
            "routerT": rT, "ATm": AT, "BTm": BT, "b2": b2,
            "ipT": ip[:, cols_ip],
            "convw": gmaj(cw[mc0:mc0 + DLOC], NGM),
            "convb": gmaj(cb[mc0:mc0 + DLOC], NGM),
            "xpTm": gmaj(xpT[mc0:mc0 + DLOC], NGM),
            "dtpT": dtpT[:, mc0:mc0 + DLOC],
            "dtbias": gmaj(dtb[mc0:mc0 + DLOC], NGM),
            "alog": gmaj(alog[mc0:mc0 + DLOC], NGM),
            "dpar": gmaj(dpar[mc0:mc0 + DLOC], NGM),
            "opT": opT[mc0:mc0 + DLOC],
            "dtlnw": np.asarray(inp["L1_attn_dt_ln_w"]).reshape(2, 128).T,
            "blnw": np.asarray(inp["L1_attn_B_ln_w"]).reshape(16, 1),
            "clnw": np.asarray(inp["L1_attn_C_ln_w"]).reshape(16, 1),
        }
        m.update(consts)
        in_maps.append({k: np.ascontiguousarray(v) for k, v in m.items()})
    return in_maps


def _assemble_output(results):
    out = np.empty((1, 8192, D), np.float32)
    for c in range(NCORE):
        out[0, _token_idx(c), :] = np.asarray(results[c]["out"]).T
    return out



def build_nc():
    nc = bass.Bass("TRN2", target_bir_lowering=False)

    def dinp(name, shape, dt):
        return nc.dram_tensor(name, shape, dt, kind="ExternalInput").ap()

    x0c = dinp("x0c", [D, TLOC], F32)
    wTf = dinp("wTf", [D, D], F32)
    mTf = dinp("mTf", [D, D], I32)
    w1Tf = dinp("w1Tf", [D, D], F32)
    tTf = dinp("tTf", [D, D], I8)
    s2f = dinp("s2f", [32, 4096], F32)
    routerT = dinp("routerT", [D, 8], F32)
    ATm = dinp("ATm", [D, 128], F32)
    BTm = dinp("BTm", [128, D], F32)
    b2i = dinp("b2", [128, 32], F32)
    ipT = dinp("ipT", [D, 2 * DLOC], F32)
    convw = dinp("convw", [128, NGM * 4], F32)
    convb = dinp("convb", [128, NGM], F32)
    xpTm = dinp("xpTm", [128, NGM * 288], F32)
    dtpT = dinp("dtpT", [RNK, DLOC], F32)
    dtbias = dinp("dtbias", [128, NGM], F32)
    alog = dinp("alog", [128, NGM * NST], F32)
    dpar = dinp("dpar", [128, NGM], F32)
    opT = dinp("opT", [DLOC, D], F32)
    dtlnw = dinp("dtlnw", [128, 2], F32)
    blnw = dinp("blnw", [16, 1], F32)
    clnw = dinp("clnw", [16, 1], F32)
    cvals = dinp("cvals", [128, 6], F32)
    ones_col_f = dinp("ones_col_f", [128, 1], F32)
    ones_row_f = dinp("ones_row_f", [1, 128], F32)
    ones16_f = dinp("ones16_f", [16, 1], F32)
    sel16b = dinp("sel16b", [16, 16 * 128], BF16)
    pat_er = dinp("pat_er", [8, 128], F32)

    out = nc.dram_tensor("out", [D, TLOC], F32, kind="ExternalOutput").ap()
    dbg = None
    dbg2 = None
    if STAGE in ("attn", "x1s", "fulldbg", "p1dbg"):
        dbg = nc.dram_tensor("dbg", [D, 3584], F32, kind="ExternalOutput").ap()
    if STAGE == "p1dbg":
        dbg2 = nc.dram_tensor("dbg2", [128, 3 * 8192 + 128], F32,
                              kind="ExternalOutput").ap()

    wm_loc = nc.dram_tensor("wm_loc", [32, 32, 128, 128], BF16)
    w1_loc = nc.dram_tensor("w1_loc", [32, 32, 128, 128], BF16)
    wq_loc = nc.dram_tensor("wq_loc", [32, 32, 128, 128], BF16)
    ipb = nc.dram_tensor("ipb", [16, 32, 128, 128], BF16)
    opb = nc.dram_tensor("opb", [32, 8, 128, 128], BF16)
    x2sp = nc.dram_tensor("x2sp", [D, TLOC], F32)
    ag_in = [nc.dram_tensor(f"ag_in{p}", [D, BLK], BF16) for p in range(NPIECE)]
    ag_out = [nc.dram_tensor(f"ag_out{p}", [8, D, BLK], BF16)
              for p in range(NPIECE)]
    ar_in = [nc.dram_tensor(f"ar_in{p}", [288, 1024], F32)
             for p in range(NPIECE)]
    ar_out = [nc.dram_tensor(f"ar_out{p}", [288, 1024], F32)
              for p in range(NPIECE)]
    rs_in = [nc.dram_tensor(f"rs_in{p}", [8, D, BLK], F32)
             for p in range(NPIECE)]
    rs_out = [nc.dram_tensor(f"rs_out{p}", [D, BLK], F32)
              for p in range(NPIECE)]

    es = contextlib.ExitStack()
    with es:
        def sb(name, shape, dt):
            return es.enter_context(nc.sbuf_tensor(name, shape, dt))

        wbufA = sb("wbufA", [128, 32 * 128], BF16)
        wbufB = sb("wbufB", [128, 32 * 128], BF16)
        c_cv = sb("c_cv", [128, 6], F32)
        c_onesc = sb("c_onesc", [128, 1], F32)
        c_onesr = sb("c_onesr", [1, 128], F32)
        c_ones16 = sb("c_ones16", [16, 1], F32)
        c_sel = sb("c_sel", [16, 16 * 128], BF16)
        c_pat = sb("c_pat", [8, 128], F32)
        c_b2 = sb("c_b2", [128, 32], F32)
        c_convw = sb("c_convw", [128, NGM * 4], F32)
        c_convb = sb("c_convb", [128, NGM], F32)
        c_dtb = sb("c_dtb", [128, NGM], F32)
        c_dpar = sb("c_dpar", [128, NGM], F32)
        c_dtlnw = sb("c_dtlnw", [128, 2], F32)
        c_blnw = sb("c_blnw", [16, 1], F32)
        c_clnw = sb("c_clnw", [16, 1], F32)
        negA = sb("negA", [128, NGM * NST], F32)
        h_state = sb("h_state", [128, 128], F32)
        halo = sb("halo", [128, NGM * 3], BF16)
        xpTb = sb("xpTb", [128, NGM * 288], BF16)
        dtpTb = sb("dtpTb", [128, 2 * DLOC], BF16)

        aW = nc.auto_arena()
        wgf = aW.sbuf_tensor("wgf", [128, 32 * 128], F32)
        mgf = aW.sbuf_tensor("mgf", [128, 32 * 128], F32)
        msk_i = aW.sbuf_tensor("msk_i", [128, 32 * 128], I32)
        tq_i8 = aW.sbuf_tensor("tq_i8", [128, 32 * 128], I8)
        aW.allocate()
        aW.free()

        a0 = nc.auto_arena()
        x0b = a0.sbuf_tensor("x0b", [128, NGD * G], BF16)
        x1b = a0.sbuf_tensor("x1b", [128, NGD * G], BF16)
        hbuf = a0.sbuf_tensor("hbuf", [128, NGD * G], F32)
        rtb = a0.sbuf_tensor("rtb", [128, 32 * 8], BF16)
        ATb = a0.sbuf_tensor("ATb", [128, 32 * 128], BF16)
        BTb = a0.sbuf_tensor("BTb", [128, 32 * 128], BF16)
        hallb = a0.sbuf_tensor("hallb", [128, G], BF16)
        gbuf = a0.sbuf_tensor("gbuf", [128, G], BF16)
        scr1_0 = a0.sbuf_tensor("scr1_0", [128, G], F32)
        scr2_0 = a0.sbuf_tensor("scr2_0", [128, G], F32)
        srow_0 = a0.sbuf_tensor("srow_0", [33, G], F32)
        rrow = a0.sbuf_tensor("rrow", [8, 4 * G], F32)
        a0.allocate()
        a0.free()

        a1 = nc.auto_arena()
        x1g = a1.sbuf_tensor("x1g", [128, 32 * G], BF16)
        xz = a1.sbuf_tensor("xz", [128, 16 * 1024], BF16)
        xe = a1.sbuf_tensor("xe", [128, NGM * 1027], BF16)
        xcb = a1.sbuf_tensor("xcb", [128, NGM * 1024], BF16)
        dtt = a1.sbuf_tensor("dtt", [128, NGM * 1024], BF16)
        Hs = a1.sbuf_tensor("Hs", [128, 4 * 1024], BF16)
        Bb = a1.sbuf_tensor("Bb", [128, 4 * 1024], BF16)
        Cb = a1.sbuf_tensor("Cb", [128, 4 * 1024], BF16)
        Ebuf = a1.sbuf_tensor("Ebuf", [128, 2 * 1024], F32)
        ubuf = a1.sbuf_tensor("ubuf", [128, 1024], BF16)
        ssm = a1.sbuf_tensor("ssm", [128, 2 * 1024], BF16)
        bcn = a1.sbuf_tensor("bcn", [16, 2 * 1024], BF16)
        dtr = a1.sbuf_tensor("dtr", [128, 2 * 1024], BF16)
        ygat = a1.sbuf_tensor("ygat", [128, NGM * 1024], BF16)
        rs_st = a1.sbuf_tensor("rs_st", [128, G], F32)
        a1.allocate()
        a1.free()

        a2 = nc.auto_arena()
        fhbuf = a2.sbuf_tensor("fhbuf", [128, NGD * G], F32)
        fx3b = a2.sbuf_tensor("fx3b", [128, NGD * G], BF16)
        frs = a2.sbuf_tensor("frs", [128, NGD * 128], F32)
        fx2 = a2.sbuf_tensor("fx2", [128, NGD * 128], F32)
        scr1_2 = a2.sbuf_tensor("scr1_2", [128, G], F32)
        scr2_2 = a2.sbuf_tensor("scr2_2", [128, G], F32)
        srow_2 = a2.sbuf_tensor("srow_2", [33, G], F32)
        a2.allocate()

        pbk = [es.enter_context(nc.psum_tensor(f"pb{i}", [128, G], F32))
               for i in range(6)]
        p67 = es.enter_context(nc.psum_tensor("p67", [128, 1024], F32))

        sems = {k: es.enter_context(nc.semaphore(k)) for k in SEMS}
        block = es.enter_context(nc.Block())

        def script(pp: P):
            freed = {i: None for i in range(8)}

            def psum_wait_writer(bank):
                f = freed[bank]
                if f is not None:
                    pp.wait("tensor", f[0], f[1])

            def psum_set_free(bank, sem):
                freed[bank] = (sem, pp.mark(sem))

            for dst, src in [(c_cv, cvals), (c_onesc, ones_col_f),
                             (c_onesr, ones_row_f),
                             (c_ones16, ones16_f), (c_sel, sel16b),
                             (c_pat, pat_er), (c_b2, b2i), (c_convw, convw),
                             (c_convb, convb), (c_dtb, dtbias),
                             (c_dpar, dpar), (c_dtlnw, dtlnw),
                             (c_blnw, blnw), (c_clnw, clnw)]:
                pp.dma("sync", dst[:, :], src, "ld")
            pp.dma("sync", wgf[:, 0:NGM * NST], alog, "ld")
            ld_c = pp.mark("ld")
            pp.wait("scalar", "ld", ld_c)
            pp.op("scalar", "activation", mgf[:, 0:NGM * NST],
                  wgf[:, 0:NGM * NST], AF.Exp, inc="sx")
            pp.wait("vector", "sx")
            pp.op("vector", "tensor_scalar_mul", negA[:, :],
                  mgf[:, 0:NGM * NST], -1.0, inc="vx")

            pp.dma("gpsimd", xpTb[:, :], xpTm, "gld")
            pp.dma("gpsimd", dtpTb.ap().rearrange("k (rt n) -> k rt n", n=DLOC),
                   dtpT.rearrange("(rt k) n -> k rt n", k=128), "gld")
            pp.dma("gpsimd", rtb.ap().rearrange("k (kt e) -> k kt e", e=8),
                   routerT.rearrange("(kt k) e -> k kt e", k=128), "gld")
            pp.dma("gpsimd", ATb.ap().rearrange("k (kt m) -> k kt m", m=128),
                   ATm.rearrange("(kt k) m -> k kt m", k=128), "gld")
            pp.dma("gpsimd", BTb[:, :], BTm, "gld")
            pp.op("gpsimd", "memset", h_state[:, :], 0.0, inc="gx")
            pp.op("gpsimd", "memset", halo[:, :], 0.0, inc="gx")

            for oc in range(32):
                colsl = slice(oc * 128, (oc + 1) * 128)
                pp.wait("sync", "vx")
                pp.dma("sync", wgf.ap().rearrange("k (kt n) -> k kt n", n=128),
                       wTf[:, colsl].rearrange("(kt k) n -> k kt n",
                                               k=128), "ld")
                pp.dma("sync", msk_i.ap().rearrange("k (kt n) -> k kt n", n=128),
                       mTf[:, colsl].rearrange("(kt k) n -> k kt n",
                                               k=128), "ld")
                pp.wait("vector", "ld")
                pp.wait("vector", "wmd")
                pp.op("vector", "tensor_single_scalar", mgf[:, :],
                      msk_i[:, :], 0, ALU.is_gt, inc="vx")
                pp.op("vector", "tensor_mul", wbufA[:, :], wgf[:, :],
                      mgf[:, :], inc="vx")
                pp.wait("sync", "vx")
                pp.dma("sync",
                       wm_loc.ap()[oc].rearrange("kt k n -> k kt n"),
                       wbufA.ap().rearrange("k (kt n) -> k kt n", n=128),
                       "wmd")
            wm_done = pp.mark("wmd")

            for oc in range(32):
                pp.dma("gpsimd", w1_loc.ap()[oc],
                       w1Tf[:, oc * 128:(oc + 1) * 128]
                       .rearrange("(kt k) n -> kt k n", k=128), "w1d")
            w1_done = pp.mark("w1d")

            for oc in range(32):
                colsl = slice(oc * 128, (oc + 1) * 128)
                pp.wait("sync", "vx")
                pp.dma("sync", tq_i8.ap().rearrange("k (kt n) -> k kt n", n=128),
                       tTf[:, colsl].rearrange("(kt k) n -> k kt n",
                                               k=128), "ld")
                pp.dma("sync", wgf[0:1, 0:4096], s2f[oc:oc + 1, :], "ld")
                ldq = pp.mark("ld")
                pp.wait("tensor", "ld", ldq)
                for ch in range(8):
                    fs = slice(ch * 512, (ch + 1) * 512)
                    bank = 4 + (ch % 2)
                    psum_wait_writer(bank)
                    pp.mm(pbk[bank][:, :], c_onesr[:, :], wgf[0:1, fs],
                          start=True, stop=True, inc="pe")
                    pe_m = pp.mark("pe")
                    pp.wait("vector", "pe", pe_m)
                    pp.wait("vector", "wqd")
                    pp.op("vector", "tensor_mul", wbufB[:, fs],
                          tq_i8[:, fs], pbk[bank][:, :], inc="vx")
                    psum_set_free(bank, "vx")
                pp.wait("sync", "vx")
                pp.dma("sync",
                       wq_loc.ap()[oc].rearrange("kt k n -> k kt n"),
                       wbufB.ap().rearrange("k (kt n) -> k kt n", n=128),
                       "wqd")
            wq_done = pp.mark("wqd")

            for oc in range(16):
                pp.dma("gpsimd", ipb.ap()[oc],
                       ipT[:, oc * 128:(oc + 1) * 128]
                       .rearrange("(kt k) n -> kt k n", k=128), "gst")
            for oc in range(32):
                pp.dma("gpsimd", opb.ap()[oc],
                       opT[:, oc * 128:(oc + 1) * 128]
                       .rearrange("(kt k) n -> kt k n", k=128), "gst")
            gst_prep = pp.mark("gst")
            gld_prep = pp.mark("gld")

            def layernorm(buf, xb_dst, scA, scB, srw, eps, n_feat, ng=NGD):
                pp.wait("tensor", "vx")
                pp.wait("tensor", "sx")
                pp.wait("vector", "pe")
                pp.wait("scalar", "pe")
                pp.wait("scalar", "vx")
                psum_wait_writer(2)
                psum_wait_writer(3)
                sqm = [None, None]
                for g in range(ng):
                    sq = (scA, scB)[g % 2]
                    if sqm[g % 2] is not None:
                        pp.wait("scalar", "pe", sqm[g % 2])
                    pp.op("scalar", "activation", sq[:, :],
                          buf[:, g * G:(g + 1) * G], AF.Square, inc="sx")
                    sx_m = pp.mark("sx")
                    pp.mm(pbk[2][0:1, :], c_onesc[:, :],
                          buf[:, g * G:(g + 1) * G],
                          start=(g == 0), stop=(g == ng - 1), inc="pe")
                    pp.wait("tensor", "sx", sx_m)
                    pp.mm(pbk[3][0:1, :], c_onesc[:, :], sq[:, :],
                          start=(g == 0), stop=(g == ng - 1), inc="pe")
                    sqm[g % 2] = pp.mark("pe")
                pe_m = pp.mark("pe")
                pp.wait("scalar", "pe", pe_m)
                pp.op("scalar", "activation", srw[0:1, :], pbk[2][0:1, :],
                      AF.Identity, scale=c_cv[0:1, 3:4], inc="sx")
                pp.wait("vector", "sx")
                pp.op("vector", "tensor_mul", srw[32:33, :], srw[0:1, :],
                      srw[0:1, :], inc="vx")
                pp.wait("vector", "pe", pe_m)
                pp.op("vector", "scalar_tensor_tensor", scB[0:1, :],
                      pbk[3][0:1, :], 1.0 / n_feat, srw[32:33, :],
                      ALU.mult, ALU.subtract, inc="vx")
                psum_set_free(2, "vx")
                psum_set_free(3, "vx")
                pp.wait("scalar", "vx")
                pp.op("scalar", "activation", scB[0:1, :], scB[0:1, :],
                      AF.Ln, bias=c_cv[0:1, 0:1], inc="sx")
                pp.op("scalar", "activation", scB[0:1, :], scB[0:1, :],
                      AF.Exp, scale=c_cv[0:1, 2:3], inc="sx")
                sx_m = pp.mark("sx")
                psum_wait_writer(4)
                psum_wait_writer(5)
                pp.wait("tensor", "sx", sx_m)
                pp.mm(pbk[4][:, :], c_onesr[:, :], srw[0:1, :],
                      start=True, stop=True, inc="pe")
                pp.mm(pbk[5][:, :], c_onesr[:, :], scB[0:1, :],
                      start=True, stop=True, inc="pe")
                pe_m = pp.mark("pe")
                pp.wait("vector", "pe", pe_m)
                for g in range(ng):
                    gs = slice(g * G, (g + 1) * G)
                    pp.op("vector", "tensor_sub", scA[:, :], buf[:, gs],
                          pbk[4][:, :], inc="vx")
                    pp.op("vector", "tensor_mul", buf[:, gs], scA[:, :],
                          pbk[5][:, :], inc="vx")
                    if xb_dst is not None:
                        pp.wait("scalar", "vx")
                        pp.op("scalar", "copy", xb_dst[:, gs], buf[:, gs],
                              inc="sx")
                psum_set_free(4, "vx")
                psum_set_free(5, "vx")

            def mm_sweep(n_oc, n_kt, lhs_src, rhs_buf, evac, extra_mm=None):
                e_pe = pp.mark("pe")
                e_st = pp.mark("st")
                mm_m = [e_pe, e_pe]
                for oc in range(n_oc):
                    wb = (wbufA, wbufB)[oc % 2]
                    lsem = ("lda", "ldb")[oc % 2]
                    pp.wait("sync", "pe", mm_m[oc % 2])
                    if oc < 2:
                        pp.wait("sync", "st", e_st)
                    pp.dma("sync",
                           wb[:, 0:n_kt * 128]
                           .rearrange("k (kt n) -> k kt n", n=128),
                           lhs_src(oc).rearrange("kt k n -> k kt n"), lsem)
                    dma_m = pp.mark(lsem)
                    bank = oc % 2
                    psum_wait_writer(bank)
                    pp.wait("tensor", lsem, dma_m)
                    for kt in range(n_kt):
                        last = (kt == n_kt - 1) and extra_mm is None
                        pp.mm(pbk[bank][:, :], wb[:, kt * 128:(kt + 1) * 128],
                              rhs_buf[:, kt * G:(kt + 1) * G],
                              start=(kt == 0), stop=last,
                              inc="pe" if last else None)
                    if extra_mm is not None:
                        extra_mm(oc, pbk[bank])
                    mm_m[oc % 2] = pp.mark("pe")
                    pp.wait("vector", "pe", mm_m[oc % 2])
                    evac(oc, pbk[bank])
                    psum_set_free(bank, "vx")

            pp.barrier()
            agm = {}
            for hf in range(2):
                tsl = slice(hf * G, (hf + 1) * G)
                pp.wait("gpsimd", "pe")
                pp.wait("gpsimd", "st")
                pp.wait("vector", "st")
                pp.wait("scalar", "st")
                pp.dma("gpsimd", x0b.ap().rearrange("k (g t) -> k g t", t=G),
                       x0c[:, tsl].rearrange("(g k) t -> k g t", k=128),
                       "gld")
                gld_x0 = pp.mark("gld")
                pp.wait("sync", "wmd", wm_done)

                def attn_evac(oc, ps):
                    pp.op("vector", "tensor_add",
                          hbuf[:, oc * G:(oc + 1) * G], ps[:, :],
                          x0b[:, oc * G:(oc + 1) * G], inc="vx")

                pp.wait("tensor", "gld", gld_x0)
                mm_sweep(32, 32, lambda oc: wm_loc.ap()[oc],
                         x0b, attn_evac)
                if STAGE == "x1s" and hf == 1:
                    vx_d = pp.mark("vx")
                    pp.wait("gpsimd", "vx", vx_d)
                    pp.dma("gpsimd", dbg[:, 1024:1536]
                           .rearrange("(g k) t -> k g t", k=128),
                           x0b.ap().rearrange("k (g t) -> k g t", t=G), "gst")
                    pp.wait("sync", "vx", vx_d)
                    pp.dma("sync", dbg[:, 1536:2048]
                           .rearrange("(g k) t -> k g t", k=128),
                           hbuf.ap().rearrange("k (g t) -> k g t", t=G), "st")
                    pp.wait("vector", "st")
                    pp.wait("vector", "gst")

                if STAGE == "araw":
                    pp.wait("sync", "vx")
                    pp.dma("sync",
                           out[:, tsl].rearrange("(g k) t -> k g t", k=128),
                           hbuf.ap().rearrange("k (g t) -> k g t", t=G),
                           "st")
                    continue

                layernorm(hbuf, x1b, scr1_0, scr2_0, srow_0, EPS_LN, D)

                if STAGE == "x1s":
                    vx_d = pp.mark("vx")
                    pp.wait("sync", "vx", vx_d)
                    pp.dma("sync",
                           dbg[:, hf * G:(hf + 1) * G]
                           .rearrange("(g k) t -> k g t", k=128),
                           hbuf.ap().rearrange("k (g t) -> k g t", t=G),
                           "st")
                    pp.wait("vector", "st")

                if STAGE == "a0":
                    pp.wait("sync", "vx")
                    pp.dma("sync",
                           out[:, tsl].rearrange("(g k) t -> k g t", k=128),
                           hbuf.ap().rearrange("k (g t) -> k g t", t=G),
                           "st")
                    continue

                r_sl = scr1_0[0:8, :]
                r_msk = rrow[0:8, 3 * G:4 * G]
                r_e = scr2_0[0:8, :]
                r_m1 = rrow[0:1, 0:G]
                r_m2 = rrow[0:1, G:2 * G]
                r_di = rrow[0:1, 2 * G:3 * G]
                psum_wait_writer(2)
                psum_wait_writer(6)
                psum_wait_writer(7)
                sx_x1b = pp.mark("sx")
                pp.wait("tensor", "sx", sx_x1b)
                for kt in range(32):
                    pp.mm(pbk[2][0:8, :], rtb[:, kt * 8:(kt + 1) * 8],
                          x1b[:, kt * G:(kt + 1) * G],
                          start=(kt == 0), stop=(kt == 31),
                          inc="pe" if kt == 31 else None)
                for kt in range(32):
                    pp.mm(p67[:, 0:512], ATb[:, kt * 128:(kt + 1) * 128],
                          x1b[:, kt * G:(kt + 1) * G],
                          start=(kt == 0), stop=(kt == 31),
                          inc="pe" if kt == 31 else None)
                pe_r = pp.mark("pe")
                pp.wait("vector", "pe", pe_r)
                pp.op("vector", "tensor_copy", r_sl, pbk[2][0:8, :],
                      inc="vx")
                psum_set_free(2, "vx")
                vx_sl = pp.mark("vx")
                pp.wait("gpsimd", "vx", vx_sl)
                pp.op("gpsimd", "tensor_reduce", r_m1, r_sl,
                      mybir.AxisListType.C, ALU.max, inc="gx")
                gx_m1 = pp.mark("gx")
                psum_wait_writer(3)
                pp.wait("tensor", "gx", gx_m1)
                pp.mm(pbk[3][0:8, :], c_onesr[0:1, 0:8], r_m1,
                      start=True, stop=True, inc="pe")
                pe_m1b = pp.mark("pe")
                pp.wait("vector", "pe", pe_m1b)
                pp.op("vector", "tensor_tensor", r_msk, r_sl,
                      pbk[3][0:8, :], ALU.is_ge, inc="vx")
                pp.op("vector", "scalar_tensor_tensor", r_e, r_msk,
                      -1e30, r_sl, ALU.mult, ALU.add, inc="vx")
                vx_e = pp.mark("vx")
                pp.wait("gpsimd", "vx", vx_e)
                pp.op("gpsimd", "tensor_reduce", r_m2, r_e,
                      mybir.AxisListType.C, ALU.max, inc="gx")
                gx_m2 = pp.mark("gx")
                pp.wait("vector", "gx", gx_m2)
                pp.op("vector", "tensor_sub", r_di, r_m2, r_m1, inc="vx")
                pp.wait("scalar", "vx")
                pp.op("scalar", "activation", r_di, r_di, AF.Exp, inc="sx")
                pp.wait("vector", "sx")
                pp.op("vector", "tensor_scalar_add", r_di, r_di, 1.0,
                      inc="vx")
                pp.op("vector", "reciprocal", r_di, r_di, inc="vx")
                vx_di = pp.mark("vx")
                pp.wait("tensor", "vx", vx_di)
                pp.mm(pbk[3][32:40, :], c_onesr[0:1, 0:8], r_m2,
                      start=True, stop=True, inc="pe")
                pp.mm(pbk[3][64:72, :], c_onesr[0:1, 0:8], r_di,
                      start=True, stop=True, inc="pe")
                pe_b8 = pp.mark("pe")
                pp.op("vector", "tensor_sub", r_e, r_sl, pbk[3][0:8, :],
                      inc="vx")
                pp.wait("scalar", "vx")
                pp.op("scalar", "activation", r_e, r_e, AF.Exp, inc="sx")
                pp.wait("vector", "pe", pe_b8)
                pp.wait("vector", "sx")
                pp.op("vector", "tensor_tensor", r_msk, r_sl,
                      pbk[3][32:40, :], ALU.is_ge, inc="vx")
                pp.op("vector", "tensor_mul", r_e, r_e, r_msk, inc="vx")
                pp.op("vector", "tensor_mul", r_e, r_e, pbk[3][64:72, :],
                      inc="vx")
                vx_w8 = pp.mark("vx")
                psum_set_free(3, "vx")
                pp.wait("tensor", "vx", vx_w8)
                pp.mm(p67[:, 512:1024], c_pat[:, :], r_e,
                      start=True, stop=True, inc="pe")
                pe_gx = pp.mark("pe")
                pp.wait("vector", "pe", pe_gx)
                pp.op("vector", "tensor_copy", hallb[:, :], p67[:, 0:512],
                      inc="vx")
                pp.op("vector", "tensor_mul", gbuf[:, :], hallb[:, :],
                      p67[:, 512:1024], inc="vx")
                vx_gb = pp.mark("vx")
                psum_set_free(6, "vx")
                psum_set_free(7, "vx")

                def ffn_extra(oc, ps):
                    pp.wait("tensor", "vx", vx_gb)
                    pp.mm(ps[:, :], BTb[:, oc * 128:(oc + 1) * 128],
                          gbuf[:, :], start=False, stop=True, inc="pe")

                def ffn_evac(oc, ps):
                    gs = slice(oc * G, (oc + 1) * G)
                    pp.op("vector", "scalar_tensor_tensor", hbuf[:, gs],
                          ps[:, :], c_b2[:, oc:oc + 1], hbuf[:, gs],
                          ALU.add, ALU.add, inc="vx")

                pp.wait("sync", "w1d", w1_done)
                mm_sweep(32, 32, lambda oc: w1_loc.ap()[oc],
                         x1b, ffn_evac, extra_mm=ffn_extra)

                layernorm(hbuf, x0b, scr1_0, scr2_0, srow_0, EPS_LN, D)
                vx_ln2 = pp.mark("vx")
                sx_ln2 = pp.mark("sx")

                if STAGE == "l0":
                    pp.wait("sync", "vx", vx_ln2)
                    pp.dma("sync",
                           out[:, tsl].rearrange("(g k) t -> k g t", k=128),
                           hbuf.ap().rearrange("k (g t) -> k g t", t=G),
                           "st")
                    continue

                pp.wait("sync", "vx", vx_ln2)
                pp.dma("sync",
                       x2sp.ap()[:, tsl].rearrange("(g k) t -> k g t", k=128),
                       hbuf.ap().rearrange("k (g t) -> k g t", t=G), "st")
                pp.wait("sync", "sx", sx_ln2)
                for b in range(4):
                    piece = 4 * hf + b
                    pp.dma("sync",
                           ag_in[piece].ap()
                           .rearrange("(g k) j -> k g j", k=128),
                           x0b.ap().rearrange("k (g t) -> k g t", t=G)
                           [:, :, b * 128:(b + 1) * 128], "st")
                pp.drain_flag()
                pp.wait("gpsimd", "fl", pp.mark("fl"))
                for b in range(4):
                    piece = 4 * hf + b
                    pp.coll("AllGather", ALU.bypass, ag_in[piece].ap(),
                            ag_out[piece].ap())
                    agm[piece] = pp.mark("cc")

            if STAGE in ("l0", "a0", "araw"):
                pp.wait("sync", "st")
                return

            if STAGE in ("l0s", "x1s"):
                pp.barrier()
                pp.dma("sync", out[:, :], x2sp.ap()[:, :], "st")
                pp.wait("sync", "st")
                return

            pend_rs = [None]

            def flush_rs():
                if pend_rs[0] is None:
                    return
                p_, flm = pend_rs[0]
                pend_rs[0] = None
                pp.wait("gpsimd", "fl", flm)
                pp.coll("ReduceScatter", ALU.add, rs_in[p_].ap(),
                        rs_out[p_].ap())

            pp.wait("gpsimd", "gst", gst_prep)
            pp.wait("gpsimd", "gld", gld_prep)
            pp.barrier()

            scrA = Ebuf[:, 0:512]
            scrB = Ebuf[:, 512:1024]
            srw1 = Ebuf[0:1, 1024:1536]
            srw2 = Ebuf[0:1, 1536:2048]

            for p in range(NPIECE):
                pp.wait("vector", "pe")
                pp.wait("vector", "gx")
                pp.wait("scalar", "vx")
                pp.wait("scalar", "gx")
                pp.wait("gpsimd", "vx")
                pp.wait("tensor", "sx")
                pp.wait("tensor", "vx")
                pp.wait("tensor", "gx")

                e_pe = pp.mark("pe")
                mm_m = [e_pe, e_pe]
                for hf2 in range(2):
                    pp.wait("sync", "pe")
                    pp.wait("sync", "cc", agm[p])
                    for kt in range(32):
                        pp.dma("sync",
                               x1g[:, kt * G:(kt + 1) * G]
                               .rearrange("k (rr j) -> k rr j", j=128),
                               ag_out[p].ap()
                               [4 * hf2:4 * hf2 + 4,
                                kt * 128:(kt + 1) * 128, :]
                               .rearrange("r k j -> k r j"), "ldr")
                    ld_rhs = pp.mark("ldr")
                    pp.wait("tensor", "ldr", ld_rhs)
                    for oc in range(16):
                        wb = (wbufA, wbufB)[oc % 2]
                        lsem = ("lda", "ldb")[oc % 2]
                        pp.wait("sync", "pe", mm_m[oc % 2])
                        pp.dma("sync",
                               wb.ap().rearrange("k (kt n) -> k kt n", n=128),
                               ipb.ap()[oc].rearrange("kt k n -> k kt n"),
                               lsem)
                        dma_m = pp.mark(lsem)
                        bank = oc % 2
                        psum_wait_writer(bank)
                        pp.wait("tensor", lsem, dma_m)
                        for kt in range(32):
                            pp.mm(pbk[bank][:, :],
                                  wb[:, kt * 128:(kt + 1) * 128],
                                  x1g[:, kt * G:(kt + 1) * G],
                                  start=(kt == 0), stop=(kt == 31),
                                  inc="pe" if kt == 31 else None)
                        mm_m[oc % 2] = pp.mark("pe")
                        pp.wait("scalar", "pe", mm_m[oc % 2])
                        pp.op("scalar", "copy",
                              xz[:, oc * 1024 + hf2 * G:
                                 oc * 1024 + hf2 * G + G],
                              pbk[bank][:, :], inc="sx")
                        psum_set_free(bank, "sx")
                sx_xz = pp.mark("sx")

                pp.wait("vector", "sx", sx_xz)
                pp.op("vector", "tensor_copy",
                      xe.ap().rearrange("k (g t) -> k g t", t=1027)
                      [:, :, 0:3],
                      halo.ap().rearrange("k (g t) -> k g t", t=3), inc="vx")
                pp.op("vector", "tensor_copy",
                      xe.ap().rearrange("k (g t) -> k g t", t=1027)
                      [:, :, 3:1027],
                      xz.ap().rearrange("k (g t) -> k g t", t=1024)
                      [:, 0:NGM, :], inc="vx")
                pp.op("vector", "tensor_copy",
                      halo.ap().rearrange("k (g t) -> k g t", t=3),
                      xz.ap().rearrange("k (g t) -> k g t", t=1024)
                      [:, 0:NGM, 1021:1024], inc="vx")
                for gm in range(NGM):
                    base = gm * 1027
                    osl = slice(gm * 1024, (gm + 1) * 1024)
                    pp.op("vector", "tensor_scalar", xcb[:, osl],
                          xe[:, base:base + 1024],
                          c_convw[:, gm * 4:gm * 4 + 1],
                          c_convb[:, gm:gm + 1], ALU.mult, ALU.add,
                          inc="vx")
                    for k in range(1, 4):
                        pp.op("vector", "scalar_tensor_tensor", xcb[:, osl],
                              xe[:, base + k:base + k + 1024],
                              c_convw[:, gm * 4 + k:gm * 4 + k + 1],
                              xcb[:, osl], ALU.mult, ALU.add, inc="vx")
                vx_conv = pp.mark("vx")
                pp.wait("scalar", "vx", vx_conv)
                pp.op("scalar", "activation", xcb[:, :], xcb[:, :], AF.Silu,
                      inc="sx")
                pp.op("scalar", "activation",
                      xz[:, NGM * 1024:2 * NGM * 1024],
                      xz[:, NGM * 1024:2 * NGM * 1024], AF.Silu, inc="sx")
                sx_silu = pp.mark("sx")

                pp.wait("tensor", "sx", sx_silu)
                st_prev = pp.mark("st")
                for th in range(2):
                    for ec, sz in ((0, 128), (1, 128), (2, 32)):
                        bank = 2 + ec
                        psum_wait_writer(bank)
                        for gm in range(NGM):
                            pp.mm(pbk[bank][0:sz, :],
                                  xpTb[:, gm * 288 + ec * 128:
                                       gm * 288 + ec * 128 + sz],
                                  xcb[:, gm * 1024 + th * G:
                                      gm * 1024 + th * G + G],
                                  start=(gm == 0), stop=(gm == NGM - 1),
                                  inc="pe" if gm == NGM - 1 else None)
                        pe_m = pp.mark("pe")
                        pp.wait("scalar", "pe", pe_m)
                        pp.wait("scalar", "st", st_prev)
                        pp.op("scalar", "copy", rs_st[0:sz, :],
                              pbk[bank][0:sz, :], inc="sx")
                        psum_set_free(bank, "sx")
                        sx_m = pp.mark("sx")
                        pp.wait("sync", "sx", sx_m)
                        pp.dma("sync",
                               ar_in[p].ap()[ec * 128:ec * 128 + sz,
                                             th * 512:th * 512 + 512],
                               rs_st[0:sz, :], "st")
                        st_prev = pp.mark("st")
                pp.wait("gpsimd", "st", st_prev)
                pp.drain_flag()
                pp.wait("gpsimd", "fl", pp.mark("fl"))
                pp.coll("AllReduce", ALU.add, ar_in[p].ap(), ar_out[p].ap())
                cc_ar = pp.mark("cc")

                pp.wait("gpsimd", "cc", cc_ar)
                pp.dma("gpsimd",
                       ssm.ap().rearrange("k (rt t) -> k rt t", t=1024),
                       ar_out[p].ap()[0:256, :]
                       .rearrange("(rt k) t -> k rt t", k=128), "gld")
                pp.dma("gpsimd", bcn[:, 0:1024],
                       ar_out[p].ap()[256:272, :], "gld")
                pp.dma("gpsimd", bcn[:, 1024:2048],
                       ar_out[p].ap()[272:288, :], "gld")
                ld_ssm = pp.mark("gld")

                pp.wait("scalar", "gld", ld_ssm)
                for th in range(2):
                    slot = (0, 32)[th]
                    psum_wait_writer(5)
                    for rt in range(2):
                        sq = (scrA, scrB)[rt]
                        pp.wait("scalar", "pe")
                        pp.op("scalar", "activation", sq,
                              ssm[:, rt * 1024 + th * G:
                                  rt * 1024 + th * G + G],
                              AF.Square, inc="sx")
                        sx_m = pp.mark("sx")
                        pp.wait("tensor", "sx", sx_m)
                        pp.mm(pbk[5][slot:slot + 1, :], c_onesc[:, :], sq,
                              start=(rt == 0), stop=(rt == 1),
                              inc="pe" if rt == 1 else None)
                pe_m = pp.mark("pe")
                pp.wait("scalar", "pe", pe_m)
                for th, dst in ((0, srw1), (1, srw2)):
                    slot = (0, 32)[th]
                    pp.op("scalar", "activation", dst,
                          pbk[5][slot:slot + 1, :], AF.Ln,
                          scale=c_cv[0:1, 4:5], bias=c_cv[0:1, 1:2],
                          inc="sx")
                    pp.op("scalar", "activation", dst, dst, AF.Exp,
                          scale=c_cv[0:1, 2:3], inc="sx")
                sx_m = pp.mark("sx")
                psum_set_free(5, "sx")
                psum_wait_writer(6)
                psum_wait_writer(7)
                pp.wait("tensor", "sx", sx_m)
                pp.mm(p67[:, 0:512], c_onesr[:, :], srw1,
                      start=True, stop=True, inc="pe")
                pp.mm(p67[:, 512:1024], c_onesr[:, :], srw2,
                      start=True, stop=True, inc="pe")
                pe_m = pp.mark("pe")
                pp.wait("vector", "pe", pe_m)
                pp.wait("vector", "gld", ld_ssm)
                for rt in range(2):
                    for th in range(2):
                        pp.op("vector", "scalar_tensor_tensor",
                              dtr[:, rt * 1024 + th * G:
                                  rt * 1024 + th * G + G],
                              ssm[:, rt * 1024 + th * G:
                                  rt * 1024 + th * G + G],
                              c_dtlnw[:, rt:rt + 1],
                              p67[:, th * 512:th * 512 + 512],
                              ALU.mult, ALU.mult, inc="vx")
                psum_set_free(6, "vx")
                psum_set_free(7, "vx")

                for half, lnw in ((0, c_blnw), (1, c_clnw)):
                    off = half * 1024
                    for th in range(2):
                        slot = (0, 32)[th]
                        psum_wait_writer(5)
                        pp.wait("scalar", "pe")
                        pp.op("scalar", "activation", scrA[0:16, :],
                              bcn[:, off + th * G:off + th * G + G],
                              AF.Square, inc="sx")
                        sx_m = pp.mark("sx")
                        pp.wait("tensor", "sx", sx_m)
                        pp.mm(pbk[5][slot:slot + 1, :], c_ones16[:, :],
                              scrA[0:16, :], start=True, stop=True,
                              inc="pe")
                    pe_m = pp.mark("pe")
                    pp.wait("scalar", "pe", pe_m)
                    for th, dst in ((0, srw1), (1, srw2)):
                        slot = (0, 32)[th]
                        pp.op("scalar", "activation", dst,
                              pbk[5][slot:slot + 1, :], AF.Ln,
                              scale=c_cv[0:1, 5:6], bias=c_cv[0:1, 1:2],
                              inc="sx")
                        pp.op("scalar", "activation", dst, dst, AF.Exp,
                              scale=c_cv[0:1, 2:3], inc="sx")
                    sx_m = pp.mark("sx")
                    psum_set_free(5, "sx")
                    psum_wait_writer(2)
                    pp.wait("tensor", "sx", sx_m)
                    pp.mm(pbk[2][0:16, :], c_onesr[0:1, 0:16], srw1,
                          start=True, stop=True, inc="pe")
                    pp.mm(pbk[2][32:48, :], c_onesr[0:1, 0:16], srw2,
                          start=True, stop=True, inc="pe")
                    pe_m = pp.mark("pe")
                    pp.wait("vector", "pe", pe_m)
                    for th in range(2):
                        src_ps = pbk[2][0:16, :] if th == 0 \
                            else pbk[2][32:48, :]
                        pp.op("vector", "scalar_tensor_tensor",
                              bcn[:, off + th * G:off + th * G + G],
                              bcn[:, off + th * G:off + th * G + G],
                              lnw[:, :], src_ps, ALU.mult, ALU.mult,
                              inc="vx")
                    psum_set_free(2, "vx")

                for gm in range(NGM):
                    for th in range(2):
                        bank = 3 + ((gm * 2 + th) % 2)
                        psum_wait_writer(bank)
                        pp.wait("tensor", "vx")
                        for rt in range(2):
                            pp.mm(pbk[bank][:, :],
                                  dtpTb[:, rt * 1024 + gm * 128:
                                        rt * 1024 + gm * 128 + 128],
                                  dtr[:, rt * 1024 + th * G:
                                      rt * 1024 + th * G + G],
                                  start=(rt == 0), stop=(rt == 1),
                                  inc="pe" if rt == 1 else None)
                        pe_m = pp.mark("pe")
                        pp.wait("vector", "pe", pe_m)
                        pp.wait("vector", "sx")
                        pp.op("vector", "tensor_single_scalar", scrA,
                              pbk[bank][:, :], c_dtb[:, gm:gm + 1],
                              ALU.add, inc="vx")
                        psum_set_free(bank, "vx")
                        pp.wait("scalar", "vx")
                        pp.op("scalar", "activation", scrB, scrA, AF.Abs,
                              inc="sx")
                        pp.op("scalar", "activation", scrB, scrB, AF.Exp,
                              scale=-1.0, inc="sx")
                        pp.op("scalar", "activation", scrB, scrB, AF.Ln,
                              bias=1.0, inc="sx")
                        pp.wait("vector", "sx")
                        pp.op("vector", "tensor_single_scalar", scrA,
                              scrA, 0.0, ALU.max, inc="vx")
                        pp.op("vector", "tensor_add",
                              dtt[:, gm * 1024 + th * G:
                                  gm * 1024 + th * G + G],
                              scrA, scrB, inc="vx")

                pp.wait("vector", "gx")
                pp.op("vector", "tensor_mul", xe[:, 0:8192], dtt[:, 0:8192],
                      xcb[:, 0:8192], inc="vx")
                vx_dtxc = pp.mark("vx")

                flush_rs()
                if STAGE == "p1dbg" and p == 1:
                    pp.dma("gpsimd", dbg2[:, 24576:24704], h_state.ap(),
                           "gst")
                    pp.wait("vector", "gst")

                emarks = [pp.mark("vx")] * 2
                for wv in range(4):
                    vx_bcn = pp.mark("vx")
                    gx_w = pp.mark("gx")
                    for i in range(4):
                        n = wv * 4 + i
                        for half, dst in ((0, Bb), (1, Cb)):
                            psum_wait_writer(6)
                            psum_wait_writer(7)
                            pp.wait("tensor", "vx", vx_bcn)
                            pp.mm(p67[:, 0:512],
                                  c_sel[:, n * 128:(n + 1) * 128],
                                  bcn[:, half * 1024:half * 1024 + 512],
                                  start=True, stop=True, inc="pe")
                            pp.mm(p67[:, 512:1024],
                                  c_sel[:, n * 128:(n + 1) * 128],
                                  bcn[:, half * 1024 + 512:
                                      half * 1024 + 1024],
                                  start=True, stop=True, inc="pe")
                            pe_m = pp.mark("pe")
                            pp.wait("scalar", "pe", pe_m)
                            pp.wait("scalar", "gx", gx_w)
                            pp.wait("scalar", "vx", vx_bcn)
                            pp.op("scalar", "copy",
                                  dst[:, i * 1024:(i + 1) * 1024],
                                  p67[:, :], inc="sx")
                            psum_set_free(6, "sx")
                            psum_set_free(7, "sx")
                    sx_bc = pp.mark("sx")
                    pp.wait("gpsimd", "sx", sx_bc)

                    for gm in range(NGM):
                        gsl = slice(gm * 1024, (gm + 1) * 1024)
                        for i in range(4):
                            n = wv * 4 + i
                            eb = Ebuf[:, (i % 2) * 1024:
                                      (i % 2) * 1024 + 1024]
                            pp.wait("scalar", "vx", emarks[i % 2])
                            pp.op("scalar", "activation", eb, dtt[:, gsl],
                                  AF.Exp,
                                  scale=negA[:, gm * NST + n:
                                             gm * NST + n + 1], inc="sx")
                            sx_e = pp.mark("sx")
                            pp.wait("gpsimd", "vx", vx_dtxc)
                            pp.op("gpsimd", "tensor_mul", ubuf[:, :],
                                  xe[:, gm * 1024:gm * 1024 + 1024],
                                  Bb[:, i * 1024:(i + 1) * 1024], inc="gx")
                            gx_u = pp.mark("gx")
                            pp.wait("vector", "sx", sx_e)
                            pp.wait("vector", "gx", gx_u)
                            pp.op("vector", "tensor_tensor_scan",
                                  Hs[:, i * 1024:(i + 1) * 1024], eb,
                                  ubuf[:, :],
                                  h_state[:, gm * NST + n:
                                          gm * NST + n + 1],
                                  ALU.mult, ALU.add, inc="vx")
                            emarks[i % 2] = pp.mark("vx")
                            pp.wait("gpsimd", "vx", emarks[i % 2])
                            pp.wait("scalar", "vx", emarks[i % 2])
                            pp.op("scalar", "copy",
                                  h_state[:, gm * NST + n:gm * NST + n + 1],
                                  Hs[:, i * 1024 + 1023:i * 1024 + 1024],
                                  inc="sx")
                        sx_sv = pp.mark("sx")
                        pp.wait("vector", "sx", sx_sv)
                        pp.op("vector", "tensor_mul", Hs[:, :], Hs[:, :],
                              Cb[:, :], inc="vx")
                        pp.op("vector", "tensor_add", Hs[:, 0:2048],
                              Hs[:, 0:2048], Hs[:, 2048:4096], inc="vx")
                        pp.op("vector", "tensor_add", Hs[:, 0:1024],
                              Hs[:, 0:1024], Hs[:, 1024:2048], inc="vx")
                        if wv == 0:
                            pp.op("vector", "tensor_copy", ygat[:, gsl],
                                  Hs[:, 0:1024], inc="vx")
                        else:
                            pp.op("vector", "tensor_add", ygat[:, gsl],
                                  ygat[:, gsl], Hs[:, 0:1024], inc="vx")

                vx_y = pp.mark("vx")
                pp.wait("gpsimd", "vx", vx_y)
                for gm in range(NGM):
                    gsl = slice(gm * 1024, (gm + 1) * 1024)
                    pp.op("gpsimd", "tensor_single_scalar", ubuf[:, :],
                          xcb[:, gsl], c_dpar[:, gm:gm + 1], ALU.mult,
                          inc="gx")
                    pp.op("gpsimd", "tensor_add", ygat[:, gsl],
                          ygat[:, gsl], ubuf[:, :], inc="gx")
                    pp.op("gpsimd", "tensor_mul", ygat[:, gsl],
                          ygat[:, gsl],
                          xz[:, (NGM + gm) * 1024:(NGM + gm) * 1024 + 1024],
                          inc="gx")
                gx_gate = pp.mark("gx")

                if STAGE == "p1dbg" and p == 1:
                    pp.dma("gpsimd", dbg2[:, 0:8192], xcb.ap(), "gst")
                    pp.dma("gpsimd", dbg2[:, 8192:16384], dtt.ap(), "gst")
                    pp.dma("gpsimd", dbg2[:, 16384:24576], ygat.ap(), "gst")
                    pp.wait("gpsimd", "gst")

                pp.wait("tensor", "gx", gx_gate)
                e_pe = pp.mark("pe")
                opm = [e_pe, e_pe]
                pp.wait("scalar", "st")
                rst_prev = pp.mark("rst")
                for oc in range(32):
                    wb = (wbufA, wbufB)[oc % 2]
                    lsem = ("lda", "ldb")[oc % 2]
                    pp.wait("sync", "pe", opm[oc % 2])
                    pp.dma("sync",
                           wb[:, 0:8 * 128]
                           .rearrange("k (kt n) -> k kt n", n=128),
                           opb.ap()[oc].rearrange("kt k n -> k kt n"), lsem)
                    ld_m = pp.mark(lsem)
                    pp.wait("tensor", lsem, ld_m)
                    for th in range(2):
                        bank = (oc % 2) + (0 if th == 0 else 2)
                        psum_wait_writer(bank)
                        for gm in range(NGM):
                            pp.mm(pbk[bank][:, :],
                                  wb[:, gm * 128:(gm + 1) * 128],
                                  ygat[:, gm * 1024 + th * G:
                                       gm * 1024 + th * G + G],
                                  start=(gm == 0), stop=(gm == NGM - 1),
                                  inc="pe" if gm == NGM - 1 else None)
                        pe_m = pp.mark("pe")
                        if th == 1:
                            opm[oc % 2] = pe_m
                        pp.wait("scalar", "pe", pe_m)
                        pp.wait("scalar", "rst", rst_prev)
                        pp.op("scalar", "copy", rs_st[:, :],
                              pbk[bank][:, :], inc="sx")
                        psum_set_free(bank, "sx")
                        sx_m = pp.mark("sx")
                        pp.wait("sync", "sx", sx_m)
                        pp.dma("sync",
                               rs_in[p].ap()[4 * th:4 * th + 4,
                                             oc * 128:(oc + 1) * 128, :]
                               .rearrange("s f j -> f s j"),
                               rs_st.ap().rearrange("k (s j) -> k s j",
                                                    j=128), "rst")
                        rst_prev = pp.mark("rst")
                pp.wait("gpsimd", "rst", rst_prev)
                pp.drain_flag()
                pend_rs[0] = (p, pp.mark("fl"))

            flush_rs()
            cc_all = pp.mark("cc")

            pp.wait("sync", "cc", cc_all)
            pp.barrier()
            for hf in range(2):
                tsl = slice(hf * G, (hf + 1) * G)
                pp.wait("vector", "st")
                last_add = pp.mark("vx")
                for b in range(4):
                    piece = 4 * hf + b
                    pp.wait("sync", "vx", last_add)
                    pp.dma("sync",
                           frs.ap().rearrange("k (g j) -> k g j", j=128),
                           rs_out[piece].ap()
                           .rearrange("(g k) j -> k g j", k=128), "ld")
                    pp.dma("sync",
                           fx2.ap().rearrange("k (g j) -> k g j", j=128),
                           x2sp.ap()[:, piece * 128:(piece + 1) * 128]
                           .rearrange("(g k) j -> k g j", k=128), "ld")
                    ld_m = pp.mark("ld")
                    pp.wait("vector", "ld", ld_m)
                    pp.op("vector", "tensor_add",
                          fhbuf.ap().rearrange("k (g t) -> k g t", t=G)
                          [:, :, b * 128:(b + 1) * 128],
                          frs.ap().rearrange("k (g j) -> k g j", j=128),
                          fx2.ap().rearrange("k (g j) -> k g j", j=128),
                          inc="vx")
                    last_add = pp.mark("vx")

                layernorm(fhbuf, fx3b, scr1_2, scr2_2, srow_2, EPS_LN, D)
                if STAGE == "fulldbg":
                    vxm = pp.mark("vx")
                    pp.wait("sync", "vx", vxm)
                    pp.dma("sync",
                           dbg[:, hf * G:(hf + 1) * G]
                           .rearrange("(g k) t -> k g t", k=128),
                           fhbuf.ap().rearrange("k (g t) -> k g t", t=G),
                           "st")
                    pp.wait("vector", "st")
                if STAGE == "attn":
                    vxm = pp.mark("vx")
                    pp.wait("sync", "vx", vxm)
                    pp.dma("sync",
                           out[:, tsl].rearrange("(g k) t -> k g t", k=128),
                           fhbuf.ap().rearrange("k (g t) -> k g t", t=G),
                           "st")
                    if hf == 1:
                        for r_ in range(8):
                            pp.dma("gpsimd",
                                   dbg[:, r_ * 128:(r_ + 1) * 128],
                                   ag_out[1].ap()[r_], "gst")
                        pp.dma("sync", dbg[:, 1024:1152], rs_out[1].ap(),
                               "st")
                        pp.dma("sync", dbg[0:288, 2048:3072],
                               ar_out[1].ap(), "st")
                    continue

                def q_evac(oc, ps):
                    gs = slice(oc * G, (oc + 1) * G)
                    pp.op("vector", "tensor_add", fhbuf[:, gs], ps[:, :],
                          fhbuf[:, gs], inc="vx")

                sx_x3b = pp.mark("sx")
                pp.wait("tensor", "sx", sx_x3b)
                pp.wait("sync", "wqd", wq_done)
                mm_sweep(32, 32, lambda oc: wq_loc.ap()[oc],
                         fx3b, q_evac)
                if STAGE == "fulldbg":
                    base = 3072 + hf * 128
                    ldb_d = pp.mark("ldb")
                    pp.wait("gpsimd", "ldb", ldb_d)
                    pp.dma("gpsimd", dbg[:, base:base + 128]
                           .rearrange("(kt k) n -> k kt n", k=128),
                           wbufB.ap().rearrange("k (kt n) -> k kt n", n=128),
                           "gst")
                    pp.dma("gpsimd", dbg[:, base + 256:base + 384]
                           .rearrange("(kt k) n -> kt k n", k=128),
                           wq_loc.ap()[31], "gst")
                    pp.wait("sync", "gst")

                if STAGE == "fulldbg":
                    vxm = pp.mark("vx")
                    pp.wait("sync", "vx", vxm)
                    pp.dma("sync",
                           dbg[:, 1024 + hf * G:1024 + (hf + 1) * G]
                           .rearrange("(g k) t -> k g t", k=128),
                           fhbuf.ap().rearrange("k (g t) -> k g t", t=G),
                           "st")
                    pp.wait("vector", "st")
                    pp.wait("scalar", "st")

                layernorm(fhbuf, None, scr1_2, scr2_2, srow_2, EPS_LN, D)
                vxm = pp.mark("vx")
                pp.wait("sync", "vx", vxm)
                if STAGE == "fulldbg":
                    pp.dma("sync",
                           dbg[:, 2048 + hf * G:2048 + (hf + 1) * G]
                           .rearrange("(g k) t -> k g t", k=128),
                           fhbuf.ap().rearrange("k (g t) -> k g t", t=G),
                           "st")
                pp.dma("sync",
                       out[:, tsl].rearrange("(g k) t -> k g t", k=128),
                       fhbuf.ap().rearrange("k (g t) -> k g t", t=G), "st")

            if STAGE == "p1dbg":
                pp.barrier()
                for r_ in range(8):
                    pp.dma("gpsimd",
                           dbg[:, r_ * 128:(r_ + 1) * 128],
                           ag_out[1].ap()[r_], "gst")
                pp.dma("gpsimd", dbg[0:288, 1024:2048], ar_out[1].ap(),
                       "gst")
                pp.dma("gpsimd", dbg[:, 2048:2176], rs_out[1].ap(), "gst")
                pp.wait("gpsimd", "gst")

            pp.wait("sync", "st")

        for name in ENGS:
            def emit(eng, _name=name):
                engines = {k: None for k in ENGS}
                engines[_name] = eng
                script(P(_name, engines, sems))
            getattr(block, name)(emit)
        a2.free()

    return nc


def kernel(**inputs):
    in_maps = _shard_inputs(inputs)
    nc = build_nc()
    res = run_bass_kernel_spmd(nc, in_maps, core_ids=list(range(NCORE)))
    if STAGE in ("attn", "x1s", "fulldbg", "p1dbg"):
        np.save("_dbg.npy", np.stack(
            [np.asarray(res.results[c]["dbg"]) for c in range(NCORE)]))
    if STAGE == "p1dbg":
        np.save("_dbg2.npy", np.stack(
            [np.asarray(res.results[c]["dbg2"]) for c in range(NCORE)]))
    return _assemble_output(res.results)
